# version 11
# baseline (speedup 1.0000x reference)
import math
import contextlib
import numpy as np
import ml_dtypes
import concourse.bass as bass
import concourse.mybir as mybir
from concourse.bass_utils import run_bass_kernel_spmd

F32 = mybir.dt.float32
BF16 = mybir.dt.bfloat16
AF = mybir.ActivationFunctionType
ALU = mybir.AluOpType
AX = mybir.AxisListType

B, S, E, H, D, I = 4, 4096, 2048, 16, 128, 8192
ET = E // 128
NOWN = S // 2
NCHUNK = 4
SMALL_H = 3
RD = 1.0 / math.sqrt(D)
NEG = -1.0e30


def _slopes():
    start = 2.0 ** (-(2.0 ** (-(math.log2(H) - 3))))
    return [start * start ** i for i in range(H)]


SLOPES = _slopes()

_PERM = (np.arange(16)[:, None, None] * 512
         + 4 * np.arange(128)[None, None, :]
         + np.arange(4)[None, :, None]).reshape(-1)


def _host_weights(inp):
    out = {}
    Wq = np.asarray(inp["L0_attn_Wq"], np.float32)
    Wk = np.asarray(inp["L0_attn_Wk"], np.float32)
    Wv = np.asarray(inp["L0_attn_Wv"], np.float32)
    Wo = np.asarray(inp["L0_attn_Wo"], np.float32)
    for nm, W in (("wqT", Wq), ("wkT", Wk)):
        a = W.reshape(H, 128, ET, 128).transpose(0, 3, 2, 1)
        out[nm] = np.ascontiguousarray(a.reshape(H * 128, ET * 128))
    a = Wv.reshape(4, 512, ET, 128).transpose(0, 3, 2, 1)
    out["wvT"] = np.ascontiguousarray(a.reshape(4 * 128, ET * 512))
    a = Wo.T.reshape(H, 128, E)
    out["woT"] = np.ascontiguousarray(a.reshape(H * 128, E))
    for nm in ("w_gate", "w_up"):
        W = np.asarray(inp[f"L0_ffn_{nm}"], np.float32)[:, _PERM]
        a = W.reshape(ET, 128, 16, 512).transpose(2, 1, 0, 3)
        out[nm] = np.ascontiguousarray(a.reshape(16 * 128, ET * 512))
    W = np.asarray(inp["L0_ffn_w_down"], np.float32)[_PERM, :]
    a = W.reshape(64, 128, 2, 1024).transpose(2, 1, 0, 3)
    out["wd"] = np.ascontiguousarray(a.reshape(2 * 128, 64 * 1024))
    for nm in ("bq", "bk"):
        out[nm] = np.ascontiguousarray(
            np.asarray(inp[f"L0_attn_{nm}"], np.float32).reshape(H, 128).T)
    out["bv_b"] = np.ascontiguousarray(np.broadcast_to(
        np.asarray(inp["L0_attn_bv"], np.float32), (128, E)))
    out["bo_b"] = np.ascontiguousarray(np.broadcast_to(
        np.asarray(inp["L0_attn_bo"], np.float32), (128, E)))
    out["ident"] = np.eye(128, dtype=ml_dtypes.bfloat16)
    return out


def _host_tables(half):
    out = {}
    tb = np.full((13 * 4 * 32, 128), NEG, np.float64)
    for h in range(SMALL_H, H):
        for j in range(NCHUNK):
            q0 = (2 * j + half) * 512
            qC, qmax = q0 + 255, q0 + 511
            for kt in range(1024 * (j + 1) // 128):
                k = kt * 128 + np.arange(128)
                v = SLOPES[h] * (k - qC)
                v[k > qmax] = NEG
                tb[((h - 3) * 4 + j) * 32 + kt] = v
    out["tbig"] = np.ascontiguousarray(tb.T.astype(np.float32))
    ts = np.full((3 * 16 * 32, 128), NEG, np.float64)
    for h in range(SMALL_H):
        for cc in range(16):
            j, ss = cc // 4, cc % 4
            q0 = (2 * j + half) * 512 + ss * 128
            qC, qmax = q0 + 63, q0 + 127
            for kt in range(1024 * (j + 1) // 128):
                k = kt * 128 + np.arange(128)
                v = SLOPES[h] * (k - qC)
                v[k > qmax] = NEG
                ts[(h * 16 + cc) * 32 + kt] = v
    out["tsm"] = np.ascontiguousarray(ts.T.astype(np.float32))
    p = np.arange(128)[:, None]
    jq = np.arange(512)[None, :]
    m = np.zeros((8, 128, 512), np.float32)
    for s in range(8):
        dp = 128 * s - (512 if half == 1 else 0)
        m[s] = (p + dp <= jq)
    out["m512"] = np.ascontiguousarray(
        m.transpose(1, 0, 2).reshape(128, 8 * 512).astype(ml_dtypes.bfloat16))
    jq = np.arange(128)[None, :]
    m = np.zeros((4, 8, 128, 128), np.float32)
    for ss in range(4):
        for s in range(8):
            dp = 128 * (s - ss) - (512 if half == 1 else 0)
            m[ss, s] = (p + dp <= jq)
    out["m128"] = np.ascontiguousarray(
        m.transpose(2, 0, 1, 3).reshape(128, 4 * 8 * 128).astype(ml_dtypes.bfloat16))
    return out



class Sched:
    ENGS = ("tensor", "vector", "scalar", "gpsimd", "sync")

    def __init__(self, nc, es):
        self.nc = nc
        self.es = es
        self.q = {e: [] for e in self.ENGS}
        self.sems = {}
        for nm in ("te", "ve", "se", "ge"):
            self.sems[nm] = [es.enter_context(nc.semaphore(nm)), 0]

    def sem(self, nm):
        if nm not in self.sems:
            self.sems[nm] = [self.es.enter_context(self.nc.semaphore(nm)), 0]
        return nm

    def val(self, s):
        return self.sems[s][1]

    def emit(self, eng, fn):
        self.q[eng].append(fn)

    def wait(self, eng, s, v):
        if v <= 0:
            return
        assert v <= self.sems[s][1], f"wait on future {s}={v} > {self.sems[s][1]}"
        h = self.sems[s][0]
        self.emit(eng, lambda e, h=h, v=v: e.wait_ge(h, v))

    def waits(self, eng, toks):
        best = {}
        for t in toks:
            if t is None:
                continue
            s, v = t
            if v > best.get(s, 0):
                best[s] = v
        for s, v in best.items():
            self.wait(eng, s, v)

    def op(self, eng, fn, sem=None, k=1):
        if sem is None:
            self.emit(eng, lambda e, fn=fn: fn(e))
            return None
        h = self.sems[sem][0]

        def run(e, fn=fn, h=h, k=k):
            fn(e).then_inc(h, k)
        self.emit(eng, run)
        self.sems[sem][1] += k
        return (sem, self.sems[sem][1])

    def drain(self, eng):
        self.emit(eng, lambda e: e.drain())

    def barrier(self, skip=()):
        cur = {s: self.sems[s][1] for s in self.sems if s not in skip}
        for eng in self.ENGS:
            for s, v in cur.items():
                self.wait(eng, s, v)

    def finalize(self, block):
        for eng in self.ENGS:
            lst = self.q[eng]
            dec = getattr(block, eng)

            def body(e, lst=lst):
                for fn in lst:
                    fn(e)
            dec(body)



def build_program():
    nc = bass.Bass("TRN2", target_bir_lowering=False)
    es = contextlib.ExitStack()
    p = Sched(nc, es)

    def dma(eng, stream, dst, src):
        return p.op(eng, lambda e, d=dst, s=src: e.dma_start(d, s),
                    p.sem("D" + stream), 16)

    def dram_in(nm, shape, dt=F32):
        return nc.dram_tensor(nm, list(shape), dt, kind="ExternalInput").ap()

    x_own = dram_in("x_own", (NOWN, E))
    wqT = dram_in("wqT", (H * 128, ET * 128))
    wkT = dram_in("wkT", (H * 128, ET * 128))
    wvT = dram_in("wvT", (4 * 128, ET * 512))
    woT = dram_in("woT", (H * 128, E))
    wg_h = dram_in("w_gate", (16 * 128, ET * 512))
    wu_h = dram_in("w_up", (16 * 128, ET * 512))
    wd_h = dram_in("wd", (2 * 128, 64 * 1024))
    bq_h = dram_in("bq", (128, H))
    bk_h = dram_in("bk", (128, H))
    bvb_h = dram_in("bv_b", (128, E))
    bob_h = dram_in("bo_b", (128, E))
    id_h = dram_in("ident", (128, 128), BF16)
    tbig_h = dram_in("tbig", (128, 13 * 4 * 32))
    tsm_h = dram_in("tsm", (128, 3 * 16 * 32))
    m512_h = dram_in("m512", (128, 8 * 512), BF16)
    m128_h = dram_in("m128", (128, 4 * 8 * 128), BF16)

    y_h = nc.dram_tensor("y", [NOWN, E], F32, kind="ExternalOutput").ap()

    K_own = nc.dram_tensor("K_own", [H * 128, NOWN], BF16, kind="Internal").ap()
    V_own = nc.dram_tensor("V_own", [NOWN, E], BF16, kind="Internal").ap()
    K_gth = nc.dram_tensor("K_gth", [2 * H * 128, NOWN], BF16, kind="Internal").ap()
    V_gth = nc.dram_tensor("V_gth", [2 * NOWN, E], BF16, kind="Internal").ap()
    Q_sp = nc.dram_tensor("Q_sp", [H * 128, NOWN], BF16, kind="Internal").ap()
    Hsp = nc.dram_tensor("Hsp", [I, NOWN], BF16, kind="Internal").ap()
    Fsp = nc.dram_tensor("Fsp", [NOWN, E], F32, kind="Internal").ap()

    MBASE = 16512
    MEND = 229344
    sizes = {F32: 4, BF16: 2}

    def at(nm, cols, dt, off):
        assert off >= MBASE and off + cols * sizes[dt] <= MEND, (nm, off)
        return nc.alloc_sbuf_tensor_at(nm, [128, cols], dt, offset=off)

    o = MBASE
    ident = at("identS", 128, BF16, o); o += 256
    bqc = at("bqc", H, F32, o); o += 64
    bkc = at("bkc", H, F32, o); o += 64
    tbig = at("tbigS", 1664, F32, o); o += 1664 * 4
    tsm = at("tsmS", 1536, F32, o); o += 1536 * 4
    m512 = at("m512S", 4096, BF16, o); o += 8192
    m128 = at("m128S", 4096, BF16, o); o += 8192
    O_R = (o + 31) // 32 * 32
    O_W = O_R + 131072
    assert O_W + 52000 <= MEND, O_W

    xnT_o = at("xnT_o", ET * NOWN, BF16, O_R)
    wgB = at("wgB", ET * 512, BF16, O_R + 65536)
    wuB = at("wuB", ET * 512, BF16, O_R + 65536 + 16384)
    hstg2 = [at(f"hstg{i}", 4 * NOWN, BF16, O_R + 65536 + 32768 + i * 16384)
             for i in range(2)]
    wd_s = at("wd_s", 64 * 1024, BF16, O_R)
    avT = at("avT", H * NOWN, BF16, O_R)
    o5 = O_R + 65536
    kbuf = [at(f"kbuf{i}", S, BF16, o5 + i * 8192) for i in range(2)]
    o5 += 16384
    vbuf = [at(f"vbuf{i}", 32 * 130, BF16, o5 + i * 8320) for i in range(2)]
    o5 += 16640
    qbuf = [at(f"qbuf{i}", 2048, BF16, o5 + i * 4096) for i in range(2)]
    o5 += 8192
    ptb = [at(f"ptb{i}", 512, BF16, o5 + i * 1024) for i in range(6)]
    o5 += 6144
    zin = at("zin", 16, F32, o5); o5 += 64
    woT_s = at("woT_s", H * E, BF16, O_R + 65536)

    xb = [at(f"xb{i}", E, F32, O_W + i * 8192) for i in range(4)]
    xnb2 = [at(f"xnb{i}", E, BF16, O_W + 32768 + i * 4096) for i in range(2)]
    xnb = xnb2[0]
    stats = at("stats", 128, F32, O_W + 40960)
    O_P2 = O_R + 65536
    wkb = [at(f"wkb{i}", ET * 128, BF16, O_P2 + i * 4096) for i in range(2)]
    kout = [at(f"kout{i}", S, BF16, O_P2 + 8192 + i * 8192) for i in range(2)]
    wvb = at("wvb", ET * 512, BF16, O_P2 + 24576)
    vout = [at(f"vout{i}", 512, BF16, O_P2 + 40960 + i * 1024)
            for i in range(2)]
    bvb_s = at("bvb_s", E, F32, O_P2 + 43008)
    wgA = at("wgA", ET * 512, BF16, O_W)
    wuA = at("wuA", ET * 512, BF16, O_W + 16384)
    hbuf = at("hbuf", 1536, F32, O_W + 32768)
    hsp2 = [at(f"hsp{i}", 512, BF16, O_W + 38912 + i * 1024)
            for i in range(2)]
    sps = [at(f"sp{i}", 512, F32, O_W + 40960 + i * 2048) for i in range(4)]
    spm = at("spm", 512, F32, O_W + 49152)
    htb = [at(f"htb{i}", 64 * 128, BF16, O_W + i * 16384) for i in range(2)]
    fout = at("fout", 1024, F32, O_W + 32768)
    bob16 = at("bob16", E, BF16, O_W + 36864)
    xt4 = [at(f"xt4{i}", 1024, F32, O_W + 40960 + i * 4096) for i in range(2)]
    ft = [at(f"ft{i}", E, F32, O_W + 16384 + i * 8192) for i in range(2)]
    ob = [at(f"ob{i}", E, F32, O_W + 32768 + i * 8192) for i in range(2)]

    pslot = [es.enter_context(nc.psum_tensor(f"ps{i}", [128, 512], F32))
             for i in range(4)]
    pav = [es.enter_context(nc.psum_tensor(f"pav{i}", [128, 512], F32))
           for i in range(4)]

    slot_free = [None] * 4
    slot_rr = [0]
    slot_n = [4]

    def acquire_slot(extra=()):
        s = slot_rr[0]
        slot_rr[0] = (s + 1) % slot_n[0]
        toks = [slot_free[s]] + list(extra)
        return s, toks

    c_toks = [dma("sync", "c0", ident[:], id_h[:, :]),
              dma("sync", "c1", tbig[:], tbig_h[:, :]),
              dma("sync", "c2", tsm[:], tsm_h[:, :]),
              dma("sync", "c3", m512[:], m512_h[:, :]),
              dma("sync", "c4", m128[:], m128_h[:, :])]

    def ln_phase(x_src, ntile, xnT_h, ncols):
        grp_done = []
        x_free = [None] * 4
        xn_free = [None, None]
        for g4 in range(ntile // 4):
            dtoks = []
            for i in range(4):
                t = g4 * 4 + i
                p.waits("sync", [x_free[i]])
                dtoks.append(dma("sync", f"xb{i}", xb[i][:, :],
                                 x_src[t * 128:(t + 1) * 128, :]))
            p.drain("vector")
            for i in range(4):
                p.waits("vector", [dtoks[i]])
                for c in range(4):
                    p.op("vector", lambda e, i=i, c=c: e.bn_stats(
                        stats[:, i * 24 + c * 6: i * 24 + (c + 1) * 6],
                        xb[i][:, c * 512:(c + 1) * 512]))
            p.drain("vector")
            for i in range(4):
                p.op("vector", lambda e, i=i: e.bn_aggr(
                    stats[:, 96 + 2 * i: 98 + 2 * i],
                    stats[:, i * 24:(i + 1) * 24]))
            p.drain("vector")
            mean = stats[:, 96:104:2]
            var = stats[:, 97:104:2]
            A = stats[:, 104:108]
            R0 = stats[:, 108:112]
            T = stats[:, 112:116]
            RS = stats[:, 116:120]
            NM = stats[:, 120:124]
            SQ = stats[:, 124:128]
            p.op("vector", lambda e: e.tensor_scalar(A, var, 1e-5, None, ALU.add))
            p.drain("vector")
            rtok = p.op("vector", lambda e: e.reciprocal(R0, A), "ve")
            p.waits("scalar", [rtok])
            stok = p.op("scalar", lambda e: e.activation(SQ, R0, AF.Sqrt), "se")
            p.waits("vector", [stok])
            p.op("vector", lambda e: e.tensor_tensor(T, SQ, SQ, ALU.mult))
            p.drain("vector")
            p.op("vector", lambda e: e.tensor_tensor(T, T, A, ALU.mult))
            p.drain("vector")
            p.op("vector", lambda e: e.tensor_scalar(T, T, -0.5, 1.5,
                                                     ALU.mult, ALU.add))
            p.drain("vector")
            p.op("vector", lambda e: e.tensor_tensor(RS, SQ, T, ALU.mult))
            p.drain("vector")
            p.op("vector", lambda e: e.scalar_tensor_tensor(
                NM, mean, -1.0, RS, ALU.mult, ALU.mult))
            p.drain("vector")
            for i in range(4):
                t = g4 * 4 + i
                sl2 = i % 2
                p.waits("vector", [xn_free[sl2]])
                xn_tok = p.op("vector", lambda e, i=i, sl2=sl2: e.tensor_scalar(
                    xnb2[sl2][:], xb[i][:], RS[:, i:i + 1], NM[:, i:i + 1],
                    ALU.mult, ALU.add), "ve")
                x_free[i] = xn_tok
                for g in range(4):
                    s, toks = acquire_slot([xn_tok, c_toks[0]])
                    p.waits("tensor", toks)
                    mt = None
                    for ii in range(4):
                        et = g * 4 + ii
                        mt = p.op("tensor", lambda e, s=s, ii=ii, et=et, sl2=sl2:
                                  e.matmul(
                            pslot[s][:, ii * 128:(ii + 1) * 128],
                            xnb2[sl2][:, et * 128:(et + 1) * 128], ident[:],
                            start=True, stop=True),
                            "te" if ii == 3 else None)
                    p.waits("scalar", [mt])
                    dst = bass.AP(xnT_h, (g * 4) * ncols + t * 128,
                                  [[ET * ncols, 128], [ncols, 4], [1, 128]])
                    ev = p.op("scalar", lambda e, s=s, d=dst: e.copy(
                        d, pslot[s][:]), "se")
                    slot_free[s] = ev
                xn_free[sl2] = ("te", p.val("te"))
            grp_done.append(("se", p.val("se")))
        return grp_done

    ln_grp = ln_phase(x_own, NOWN // 128, xnT_o, NOWN)

    ck = dma("sync", "cbk", bkc[:, :], bk_h[:, :])
    cv = dma("sync", "cbv", bvb_s[:, :], bvb_h[:, :])

    wk_tok = [None, None]
    wk_used = [None, None]
    kout_free = [None, None]
    for h in range(H):
        sl = h % 2
        p.waits("gpsimd", [wk_used[sl]])
        wk_tok[sl] = dma("gpsimd", f"wk{sl}", wkb[sl][:, :], wkT[h * 128:(h + 1) * 128, :])
        evs = []
        for c8 in range(4):
            s, toks = acquire_slot([wk_tok[sl], ln_grp[c8]])
            p.waits("tensor", toks)
            mm = None
            for et in range(ET):
                mm = p.op("tensor", lambda e, s=s, et=et, c8=c8, sl=sl: e.matmul(
                    pslot[s][:], wkb[sl][:, et * 128:(et + 1) * 128],
                    xnT_o[:, et * NOWN + c8 * 512: et * NOWN + c8 * 512 + 512],
                    start=(et == 0), stop=(et == ET - 1)),
                    "te" if et == ET - 1 else None)
            w = [mm, ck] + ([kout_free[sl]] if c8 == 0 else [])
            p.waits("vector", w)
            ev = p.op("vector", lambda e, s=s, sl=sl, c8=c8, h=h: e.tensor_scalar(
                kout[sl][:, c8 * 512:(c8 + 1) * 512], pslot[s][:],
                bkc[:, h:h + 1], None, ALU.add), "ve")
            slot_free[s] = ev
            evs.append(ev)
        wk_used[sl] = ("te", p.val("te"))
        p.waits("scalar", evs)
        kout_free[sl] = dma("scalar", f"ko{sl}",
                            K_own[h * 128:(h + 1) * 128, :],
                            kout[sl][:, 0:NOWN])

    wv_used = None
    vout_free = [None, None]
    for g4 in range(4):
        p.waits("gpsimd", [wv_used])
        wv_tok = dma("gpsimd", "wv", wvb[:, :], wvT[g4 * 128:(g4 + 1) * 128, :])
        for tt in range(16):
            vsl = tt % 2
            s, toks = acquire_slot([wv_tok, ln_grp[tt // 4]])
            p.waits("tensor", toks)
            mm = None
            for et in range(ET):
                mm = p.op("tensor", lambda e, s=s, et=et, tt=tt: e.matmul(
                    pslot[s][:],
                    xnT_o[:, et * NOWN + tt * 128: et * NOWN + tt * 128 + 128],
                    wvb[:, et * 512:(et + 1) * 512],
                    start=(et == 0), stop=(et == ET - 1)),
                    "te" if et == ET - 1 else None)
            p.waits("vector", [mm, cv, vout_free[vsl]])
            ev = p.op("vector", lambda e, s=s, vsl=vsl, g4=g4: e.tensor_tensor(
                vout[vsl][:], pslot[s][:],
                bvb_s[:, g4 * 512:(g4 + 1) * 512], ALU.add), "ve")
            slot_free[s] = ev
            p.waits("scalar", [ev])
            vout_free[vsl] = dma(
                "scalar", f"vo{vsl}",
                V_own[tt * 128:(tt + 1) * 128, g4 * 512:(g4 + 1) * 512],
                vout[vsl][:, :])
        wv_used = ("te", p.val("te"))

    cq = dma("sync", "cbq", bqc[:, :], bq_h[:, :])
    wq_tok = [None, None]
    wq_used = list(wk_used)
    qout_free = list(kout_free)
    for h in range(H):
        sl = h % 2
        p.waits("gpsimd", [wq_used[sl]])
        wq_tok[sl] = dma("gpsimd", f"wk{sl}", wkb[sl][:, :],
                         wqT[h * 128:(h + 1) * 128, :])
        evs = []
        for c4 in range(4):
            s, toks = acquire_slot([wq_tok[sl], ln_grp[c4]])
            p.waits("tensor", toks)
            mm = None
            for et in range(ET):
                mm = p.op("tensor", lambda e, s=s, et=et, c4=c4, sl=sl: e.matmul(
                    pslot[s][:], wkb[sl][:, et * 128:(et + 1) * 128],
                    xnT_o[:, et * NOWN + c4 * 512: et * NOWN + c4 * 512 + 512],
                    start=(et == 0), stop=(et == ET - 1)),
                    "te" if et == ET - 1 else None)
            w = [mm, cq] + ([qout_free[sl]] if c4 == 0 else [])
            p.waits("vector", w)
            ev = p.op("vector", lambda e, s=s, sl=sl, c4=c4, h=h: e.tensor_scalar(
                kout[sl][:, c4 * 512:(c4 + 1) * 512], pslot[s][:],
                bqc[:, h:h + 1], RD, ALU.add, ALU.mult), "ve")
            slot_free[s] = ev
            evs.append(ev)
        wq_used[sl] = ("te", p.val("te"))
        p.waits("scalar", evs)
        qout_free[sl] = dma("scalar", f"ko{sl}",
                            Q_sp[h * 128:(h + 1) * 128, :], kout[sl][:, 0:NOWN])
    p.waits("gpsimd", [("D" + s_, p.val("D" + s_))
                       for s_ in ("ko0", "ko1", "vo0", "vo1")])
    cc_sem = p.sem("cc")
    rg = [[0, 1], [2, 3], [4, 5], [6, 7]]
    cc_tok = None
    for cin_, cout_ in ((K_own, K_gth), (V_own, V_gth)):
        cc_tok = p.op("gpsimd", lambda e, a=cin_, b=cout_, rg=rg:
                      e.collective_compute(
            "AllGather", ALU.bypass, replica_groups=rg,
            ins=[a.opt()], outs=[b.opt()]), "cc", 1)
    p.barrier(skip=("cc",))

    wg_bufs = [(wgA, wuA), (wgB, wuB)]
    wg_used = [None, None]
    hstg_free = [None, None]
    hsp_free = [None, None]
    pav_free3 = [None, None]
    stage_evs = {}
    pend3 = [None]

    def p3_flush():
        if pend3[0] is None:
            return
        sp_tok, ic_, tt_ = pend3[0]
        pend3[0] = None
        b3 = tt_ % 2
        p.waits("tensor", [sp_tok, pav_free3[b3]])
        mt = None
        for i in range(4):
            mt = p.op("tensor", lambda e, b3=b3, i=i, tt_=tt_: e.matmul(
                pav[b3][:, i * 128:(i + 1) * 128],
                hsp2[tt_ % 2][:, i * 128:(i + 1) * 128], ident[:],
                start=True, stop=True), "te" if i == 3 else None)
        hsp_free[tt_ % 2] = ("te", p.val("te"))
        w = [mt] + ([hstg_free[ic_ % 2]] if tt_ == 0 else [])
        p.waits("scalar", w)
        dst = bass.AP(hstg2[ic_ % 2], tt_ * 128,
                      [[4 * NOWN, 128], [NOWN, 4], [1, 128]])
        ev = p.op("scalar", lambda e, b3=b3, d=dst: e.copy(d, pav[b3][:]), "se")
        pav_free3[b3] = ev
        stage_evs.setdefault(ic_, []).append(ev)
        if tt_ == 15:
            p.waits("sync", stage_evs[ic_])
            hdst = bass.AP(Hsp.tensor, (ic_ * 512) * NOWN,
                           [[NOWN, 128], [128 * NOWN, 4], [1, NOWN]])
            hstg_free[ic_ % 2] = dma("sync", f"hs{ic_ % 2}", hdst,
                                     hstg2[ic_ % 2][:, :])

    for ic in range(16):
        sl = ic % 2
        p.waits("gpsimd", [wg_used[sl]])
        t1 = dma("gpsimd", f"wg{sl}", wg_bufs[sl][0][:, :],
                 wg_h[ic * 128:(ic + 1) * 128, :])
        t2 = dma("gpsimd", f"wg{sl}", wg_bufs[sl][1][:, :],
                 wu_h[ic * 128:(ic + 1) * 128, :])
        for tt in range(16):
            sg = tt % 2
            su = 2 + tt % 2
            p.waits("tensor", [slot_free[sg], t2])
            mg = None
            for et in range(ET):
                mg = p.op("tensor", lambda e, sg=sg, et=et, tt=tt, sl=sl: e.matmul(
                    pslot[sg][:],
                    xnT_o[:, et * NOWN + tt * 128: et * NOWN + tt * 128 + 128],
                    wg_bufs[sl][0][:, et * 512:(et + 1) * 512],
                    start=(et == 0), stop=(et == ET - 1)),
                    "te" if et == ET - 1 else None)
            p.waits("tensor", [slot_free[su]])
            mu_ = None
            for et in range(ET):
                mu_ = p.op("tensor", lambda e, su=su, et=et, tt=tt, sl=sl: e.matmul(
                    pslot[su][:],
                    xnT_o[:, et * NOWN + tt * 128: et * NOWN + tt * 128 + 128],
                    wg_bufs[sl][1][:, et * 512:(et + 1) * 512],
                    start=(et == 0), stop=(et == ET - 1)),
                    "te" if et == ET - 1 else None)
            p3_flush()
            p.waits("scalar", [mg])
            sg_tok = p.op("scalar", lambda e, sg=sg: e.activation(
                hbuf[:, 0:512], pslot[sg][:], AF.Silu), "se")
            slot_free[sg] = sg_tok
            p.waits("vector", [sg_tok, mu_])
            p.drain("vector")
            hm = p.op("vector", lambda e, su=su: e.tensor_tensor(
                hbuf[:, 512:1024], hbuf[:, 0:512], pslot[su][:], ALU.mult),
                "ve")
            slot_free[su] = hm
            A = [hbuf[:, 512 + j * 128: 512 + (j + 1) * 128] for j in range(4)]
            p.waits("scalar", [hm])
            ab = p.op("scalar", lambda e: e.activation(
                hbuf[:, 1024:1536], hbuf[:, 512:1024], AF.Abs), "se")
            Aa = [hbuf[:, 1024 + j * 128: 1024 + (j + 1) * 128] for j in range(4)]
            p.waits("vector", [ab])
            p.drain("vector")
            p.op("vector", lambda e, A0=Aa[0], A1=Aa[1]: e.tensor_tensor(
                sps[0][:, 0:128], A0, A1, ALU.max))
            p.op("vector", lambda e, A0=Aa[0], A1=Aa[1]: e.tensor_tensor(
                sps[1][:, 0:128], A0, A1, ALU.min))
            p.op("vector", lambda e, A2=Aa[2], A3=Aa[3]: e.tensor_tensor(
                sps[2][:, 0:128], A2, A3, ALU.max))
            p.op("vector", lambda e, A2=Aa[2], A3=Aa[3]: e.tensor_tensor(
                sps[3][:, 0:128], A2, A3, ALU.min))
            p.drain("vector")
            p.op("vector", lambda e: e.tensor_tensor(
                sps[0][:, 128:256], sps[0][:, 0:128], sps[2][:, 0:128],
                ALU.min))
            p.op("vector", lambda e: e.tensor_tensor(
                sps[1][:, 128:256], sps[1][:, 0:128], sps[3][:, 0:128],
                ALU.max))
            p.drain("vector")
            p.op("vector", lambda e: e.tensor_tensor(
                sps[0][:, 256:384], sps[0][:, 128:256], sps[1][:, 128:256],
                ALU.max))
            p.drain("vector")
            sec = sps[0][:, 256:384]
            for j in range(4):
                p.op("vector", lambda e, j=j, a=Aa[j], sec=sec: e.tensor_tensor(
                    spm[:, j * 128:(j + 1) * 128], a, sec, ALU.is_ge))
            p.drain("vector")
            p.waits("vector", [hsp_free[tt % 2]])
            sp_tok = None
            for j in range(4):
                sp_tok = p.op("vector", lambda e, j=j, a=A[j], tt=tt:
                              e.tensor_tensor(
                    hsp2[tt % 2][:, j * 128:(j + 1) * 128], a,
                    spm[:, j * 128:(j + 1) * 128], ALU.mult),
                    "ve" if j == 3 else None)
            pend3[0] = (sp_tok, ic, tt)
        wg_used[sl] = ("te", p.val("te"))
    p3_flush()
    wd_pre = []
    p.waits("gpsimd", [("te", p.val("te"))])
    for b in range(8):
        if b >= 2:
            p.waits("gpsimd", [wd_pre[b - 2]])
        if b == 6:
            p.waits("gpsimd", [hstg_free[0], hstg_free[1]])
        wd_pre.append(dma("gpsimd", f"wd{b}",
                          wd_s[:, b * 8192:(b + 1) * 8192],
                          wd_h[0:128, b * 8192:(b + 1) * 8192]))
    p.barrier(skip=tuple(f"Dwd{b}" for b in range(8)))

    bt = dma("gpsimd", "cbo", bob16[:, :], bob_h[:, :])
    ht_used = [None, None]
    xt4_used = [None, None]
    fout_free = None
    for half_e in range(2):
        if half_e:
            p.waits("gpsimd", [("te", p.val("te"))])
            wtoks = []
            for b in range(8):
                if b >= 2:
                    p.waits("gpsimd", [wtoks[b - 2]])
                wtoks.append(dma("gpsimd", f"wd{b}",
                                 wd_s[:, b * 8192:(b + 1) * 8192],
                                 wd_h[128:256, b * 8192:(b + 1) * 8192]))
        else:
            wtoks = wd_pre
        for tt in range(16):
            sl = tt % 2
            p.waits("sync", [ht_used[sl], xt4_used[sl]])
            xtok = dma("sync", f"x4{sl}", xt4[sl][:, :],
                       x_own[tt * 128:(tt + 1) * 128,
                             half_e * 1024:(half_e + 1) * 1024])
            toks = []
            for pc in range(8):
                srcp = bass.AP(Hsp.tensor, (pc * 8 * 128) * NOWN + tt * 128,
                               [[NOWN, 128], [128 * NOWN, 8], [1, 128]])
                dstp = htb[sl][:, pc * 8 * 128:(pc + 1) * 8 * 128]
                toks.append(dma("sync", f"ht{sl}", dstp, srcp))
            evs = []
            for ec in range(2):
                s, stoks = acquire_slot([toks[-1], wtoks[0]])
                p.waits("tensor", stoks)
                mm = None
                for it in range(64):
                    if it % 8 == 0 and it > 0:
                        p.waits("tensor", [wtoks[it // 8]])
                    mm = p.op("tensor", lambda e, s=s, it=it, ec=ec, sl=sl: e.matmul(
                        pslot[s][:], htb[sl][:, it * 128:(it + 1) * 128],
                        wd_s[:, it * 1024 + ec * 512: it * 1024 + ec * 512 + 512],
                        start=(it == 0), stop=(it == 63)),
                        "te" if it == 63 else None)
                w = [mm, bt, xtok] + ([fout_free] if ec == 0 else [])
                p.waits("vector", w)
                p.op("vector", lambda e, s=s, ec=ec, he=half_e:
                     e.tensor_tensor(
                    fout[:, ec * 512:(ec + 1) * 512], pslot[s][:],
                    bob16[:, he * 1024 + ec * 512: he * 1024 + ec * 512 + 512],
                    ALU.add))
                p.drain("vector")
                ev = p.op("vector", lambda e, sl=sl, ec=ec: e.tensor_tensor(
                    fout[:, ec * 512:(ec + 1) * 512],
                    fout[:, ec * 512:(ec + 1) * 512],
                    xt4[sl][:, ec * 512:(ec + 1) * 512], ALU.add), "ve")
                slot_free[s] = ev
                evs.append(ev)
            ht_used[sl] = ("te", p.val("te"))
            xt4_used[sl] = evs[-1]
            p.waits("scalar", evs)
            fout_free = dma(
                "scalar", "fo",
                Fsp[tt * 128:(tt + 1) * 128,
                    half_e * 1024:(half_e + 1) * 1024],
                fout[:, 0:1024])
    p.barrier()

    def win_start(h, q0v, small):
        W = math.ceil(60.0 / SLOPES[h])
        kt0 = max(0, (q0v - W) // 128)
        if small:
            kt0 &= ~3
        return kt0

    units = []
    for h in range(SMALL_H, H):
        for j in range(NCHUNK):
            units.append((h, j, 512, 1024 * (j + 1),
                          tbig, ((h - 3) * 4 + j) * 32, j * 512,
                          win_start(h, 1024 * j, False)))
    for h in range(SMALL_H):
        for cc in range(16):
            j = cc // 4
            units.append((h, cc, 128, 1024 * (j + 1),
                          tsm, (h * 16 + cc) * 32, cc * 128,
                          win_start(h, 1024 * j + 128 * (cc % 4), True)))
    slot_n[0] = 3
    slot_rr[0] = 0
    kv_used = [None, None]
    av_free = [None] * 4
    zin_free = [None] * 4
    xnb_free = [None] * 4
    ptb_free = [None] * 6
    ptsl_rr = [0]
    pending_norm = []
    pend_g = []

    def pump():
        fn, unit_first, fin = pend_g.pop(0)
        if unit_first:
            while pending_norm:
                pending_norm.pop(0)()
        fn()
        if fin is not None:
            fin()

    heads = []
    for u in units:
        if not heads or heads[-1][0] != u[0]:
            heads.append((u[0], []))
        heads[-1][1].append(u)
    for hi, (h, hunits) in enumerate(heads):
        sl = hi % 2
        p.waits("sync", [kv_used[sl], ("cc", p.val("cc"))])
        for g in range(S // 512):
            r, jj = g % 2, g // 2
            dma("sync", f"kv{sl}",
                kbuf[sl][:, g * 512:(g + 1) * 512],
                K_gth[r * 2048 + h * 128: r * 2048 + (h + 1) * 128,
                      jj * 512:(jj + 1) * 512])
            srcp = bass.AP(V_gth.tensor,
                           (r * 2048 + jj * 512) * E + h * 128,
                           [[E, 128], [128 * E, 4], [1, 128]])
            dstp = bass.AP(vbuf[sl], (g * 4) * 130,
                           [[32 * 130, 128], [130, 4], [1, 128]])
            dma("sync", f"kv{sl}", dstp, srcp)
        qtok = dma("sync", f"kv{sl}", qbuf[sl][:, :],
                   Q_sp[h * 128:(h + 1) * 128, :])
        p.waits("gpsimd", [qtok])
        ones_tok = p.op("gpsimd", lambda e, sl=sl: e.memset(
            bass.AP(vbuf[sl], 128, [[32 * 130, 128], [130, 32], [1, 1]]),
            1.0), "ge")
        for ux, (h_, cj, cw, kend, tb_s, tcol, ocol, kt0) in enumerate(hunits):
            nkt = kend // 128
            nqs = cw // 128
            if cw == 512:
                kt_groups = [[kt] for kt in range(kt0, nkt)]
            else:
                kt_groups = [list(range(g * 4, g * 4 + 4))
                             for g in range(kt0 // 4, nkt // 4)]
            first_av = [True]

            def mk_flush(grp_work, sl=sl, nqs=nqs, nkt=nkt, kt0=kt0,
                         qtok=qtok, ones_tok=ones_tok, first_av=first_av):
                def flushfn():
                    for kt, ptsl_, off, rdy in grp_work:
                        w = [rdy]
                        if first_av[0]:
                            w += [ones_tok, qtok] + av_free[0:nqs]
                            first_av[0] = False
                        p.waits("tensor", w)
                        for qs in range(nqs):
                            p.op("tensor", lambda e, qs=qs, ptsl=ptsl_, off=off,
                                 kt=kt, first=(kt == kt0), last=(kt == nkt - 1):
                                 e.matmul(
                                pav[qs][:, 0:129],
                                ptb[ptsl][:, off + qs * 128: off + qs * 128 + 128],
                                vbuf[sl][:, kt * 130:kt * 130 + 129],
                                start=first, stop=last),
                                "te" if qs == nqs - 1 else None)
                        ptb_free[ptsl_] = ("te", p.val("te"))
                return flushfn

            def make_norm_qs(qs, h=h, ocol=ocol):
                def normfn(av_fin):
                    p.waits("vector", [av_fin, zin_free[qs]])
                    rtok = p.op("vector", lambda e, qs=qs: e.reciprocal(
                        zin[:, qs:qs + 1],
                        pav[qs][:, 128:129]), "ve")
                    p.waits("vector", [xnb_free[qs]])
                    p.drain("vector")
                    ntok = p.op("vector", lambda e, qs=qs: e.tensor_scalar(
                        xnb[:, qs * 128:(qs + 1) * 128], pav[qs][:, 0:128],
                        zin[:, qs:qs + 1], None, ALU.mult), "ve")
                    zin_free[qs] = ntok
                    av_free[qs] = ntok
                    p.waits("tensor", [ntok, slot_free[3]])
                    mt = p.op("tensor", lambda e, qs=qs: e.matmul(
                        pslot[3][:, 0:128], xnb[:, qs * 128:(qs + 1) * 128],
                        ident[:], start=True, stop=True), "te")
                    xnb_free[qs] = mt
                    p.waits("vector", [mt])
                    ev = p.op("vector", lambda e, h=h, ocol=ocol, qs=qs:
                              e.tensor_copy(
                        avT[:, h * NOWN + ocol + qs * 128:
                            h * NOWN + ocol + (qs + 1) * 128],
                        pslot[3][:, 0:128]), "ve")
                    slot_free[3] = ev
                return normfn

            def mk_fin(nqs=nqs, sl=sl, head_last=(ux == len(hunits) - 1),
                       make_norm_qs=make_norm_qs):
                def fin():
                    av_fin = ("te", p.val("te"))
                    for qs_ in range(nqs):
                        fnq = make_norm_qs(qs_)
                        pending_norm.append(
                            lambda fnq=fnq, av_fin=av_fin: fnq(av_fin))
                    if head_last:
                        kv_used[sl] = ("te", p.val("te"))
                return fin

            for gi, grp in enumerate(kt_groups):
                s, toks = acquire_slot([qtok])
                p.waits("tensor", toks)
                mm = None
                for kt in grp:
                    off = (kt - grp[0]) * cw if cw == 128 else 0
                    mm = p.op("tensor", lambda e, s=s, kt=kt, off=off, sl=sl,
                              cw=cw, ocol=ocol: e.matmul(
                        pslot[s][:, off:off + cw],
                        kbuf[sl][:, kt * 128:(kt + 1) * 128],
                        qbuf[sl][:, ocol:ocol + cw], start=True, stop=True),
                        "te" if kt == grp[-1] else None)
                if pending_norm:
                    pending_norm.pop(0)()
                if len(pend_g) >= 4:
                    pump()
                ptsl = ptsl_rr[0]
                ptsl_rr[0] = (ptsl + 1) % 6
                p.waits("scalar", [mm, ptb_free[ptsl], c_toks[1], c_toks[2]])
                kt_ready = {}
                kt_off = {}
                for kt in grp:
                    off = (kt - grp[0]) * cw if cw == 128 else 0
                    etok = p.op("scalar", lambda e, s=s, kt=kt, off=off, ptsl=ptsl,
                                tb_s=tb_s, tcol=tcol, cw=cw: e.activation(
                        ptb[ptsl][:, off:off + cw], pslot[s][:, off:off + cw],
                        AF.Exp, bias=tb_s[:, tcol + kt:tcol + kt + 1], scale=1.0),
                        "se")
                    kt_ready[kt] = etok
                    kt_off[kt] = (ptsl, off)
                slot_free[s] = kt_ready[grp[-1]]
                for kt in grp:
                    msl = kt - (nkt - 8)
                    if msl >= 0:
                        ptsl_, off = kt_off[kt]
                        if cw == 512:
                            mv = m512[:, msl * 512:(msl + 1) * 512]
                        else:
                            ss = cj % 4
                            mv = m128[:, (ss * 8 + msl) * 128:(ss * 8 + msl + 1) * 128]
                        p.waits("vector", [kt_ready[kt], c_toks[3], c_toks[4]])
                        kt_ready[kt] = p.op("vector", lambda e, ptsl=ptsl_, off=off,
                                            cw=cw, mv=mv: e.tensor_tensor(
                            ptb[ptsl][:, off:off + cw], ptb[ptsl][:, off:off + cw],
                            mv, ALU.mult), "ve")
                work = [(kt, kt_off[kt][0], kt_off[kt][1], kt_ready[kt])
                        for kt in grp]
                pend_g.append((mk_flush(work), gi == 0,
                               mk_fin() if gi == len(kt_groups) - 1 else None))
    while pend_g:
        pump()
    while pending_norm:
        pending_norm.pop(0)()
    slot_n[0] = 4
    slot_rr[0] = 0
    p.barrier()

    wo_toks = []
    for hg in range(4):
        srcp = bass.AP(woT.tensor, (hg * 4 * 128) * E,
                       [[E, 128], [128 * E, 4], [1, E]])
        wo_toks.append(dma("gpsimd", "wo", woT_s[:, hg * 4 * E:(hg + 1) * 4 * E],
                           srcp))
    ft_used = [None, None]
    ob_free = [None, None]
    out_dmas = []
    for qt in range(16):
        sl = qt % 2
        p.waits("sync", [ft_used[sl]])
        ftok = dma("sync", f"ft{sl}", ft[sl][:, :],
                   Fsp[qt * 128:(qt + 1) * 128, :])
        evs = []
        for ec in range(4):
            s, toks = acquire_slot([wo_toks[-1]])
            p.waits("tensor", toks)
            mm = None
            for h in range(H):
                mm = p.op("tensor", lambda e, s=s, h=h, qt=qt, ec=ec: e.matmul(
                    pslot[s][:],
                    avT[:, h * NOWN + qt * 128: h * NOWN + (qt + 1) * 128],
                    woT_s[:, h * E + ec * 512: h * E + ec * 512 + 512],
                    start=(h == 0), stop=(h == H - 1)),
                    "te" if h == H - 1 else None)
            w = [mm, ftok] + ([ob_free[sl]] if ec == 0 else [])
            p.waits("vector", w)
            ev = p.op("vector", lambda e, s=s, sl=sl, ec=ec: e.tensor_tensor(
                ob[sl][:, ec * 512:(ec + 1) * 512], pslot[s][:],
                ft[sl][:, ec * 512:(ec + 1) * 512], ALU.add), "ve")
            slot_free[s] = ev
            evs.append(ev)
        ft_used[sl] = evs[-1]
        p.waits("scalar", evs)
        otok = dma("scalar", f"ob{sl}", y_h[qt * 128:(qt + 1) * 128, :], ob[sl][:, :])
        ob_free[sl] = otok
        out_dmas.append(otok)
    p.waits("scalar", out_dmas)
    p.barrier()

    with nc.Block() as block:
        p.finalize(block)
    es.close()
    return nc


_CACHE = {}


def kernel(**inputs):
    if "nc" not in _CACHE:
        _CACHE["nc"] = build_program()
    nc = _CACHE["nc"]
    hw = _host_weights(inputs)
    x = np.asarray(inputs["L0_attn_x"], np.float32)
    tabs = [_host_tables(0), _host_tables(1)]
    in_maps = []
    for c in range(8):
        b, half = c // 2, c % 2
        x_own = np.concatenate(
            [x[b, (2 * j + half) * 512:(2 * j + half) * 512 + 512]
             for j in range(NCHUNK)], axis=0)
        in_maps.append({
            "x_own": np.ascontiguousarray(x_own),
            "wqT": hw["wqT"], "wkT": hw["wkT"], "wvT": hw["wvT"],
            "woT": hw["woT"],
            "w_gate": hw["w_gate"], "w_up": hw["w_up"], "wd": hw["wd"],
            "bq": hw["bq"], "bk": hw["bk"],
            "bv_b": hw["bv_b"], "bo_b": hw["bo_b"],
            "ident": hw["ident"],
            "tbig": tabs[half]["tbig"], "tsm": tabs[half]["tsm"],
            "m512": tabs[half]["m512"], "m128": tabs[half]["m128"],
        })
    res = run_bass_kernel_spmd(nc, in_maps, core_ids=list(range(8)))
    y = np.empty((B, S, E), np.float32)
    for c in range(8):
        b, half = c // 2, c % 2
        yc = np.asarray(res.results[c]["y"])
        for j in range(NCHUNK):
            g = 2 * j + half
            y[b, g * 512:(g + 1) * 512] = yc[j * 512:(j + 1) * 512]
    return y


# version 12
# speedup vs baseline: 1.0286x; 1.0286x over previous
import math
import contextlib
import numpy as np
import ml_dtypes
import concourse.bass as bass
import concourse.mybir as mybir
from concourse.bass_utils import run_bass_kernel_spmd

F32 = mybir.dt.float32
BF16 = mybir.dt.bfloat16
AF = mybir.ActivationFunctionType
ALU = mybir.AluOpType
AX = mybir.AxisListType

B, S, E, H, D, I = 4, 4096, 2048, 16, 128, 8192
ET = E // 128
NOWN = S // 2
NCHUNK = 4
SMALL_H = 3
RD = 1.0 / math.sqrt(D)
NEG = -1.0e30


def _slopes():
    start = 2.0 ** (-(2.0 ** (-(math.log2(H) - 3))))
    return [start * start ** i for i in range(H)]


SLOPES = _slopes()

_PERM = (np.arange(16)[:, None, None] * 512
         + 4 * np.arange(128)[None, None, :]
         + np.arange(4)[None, :, None]).reshape(-1)


def _host_weights(inp):
    out = {}
    Wq = np.asarray(inp["L0_attn_Wq"], np.float32)
    Wk = np.asarray(inp["L0_attn_Wk"], np.float32)
    Wv = np.asarray(inp["L0_attn_Wv"], np.float32)
    Wo = np.asarray(inp["L0_attn_Wo"], np.float32)
    for nm, W in (("wqT", Wq), ("wkT", Wk)):
        a = W.reshape(H, 128, ET, 128).transpose(0, 3, 2, 1)
        out[nm] = np.ascontiguousarray(a.reshape(H * 128, ET * 128))
    a = Wv.reshape(4, 512, ET, 128).transpose(0, 3, 2, 1)
    out["wvT"] = np.ascontiguousarray(a.reshape(4 * 128, ET * 512))
    a = Wo.T.reshape(H, 128, E)
    out["woT"] = np.ascontiguousarray(a.reshape(H * 128, E))
    for nm in ("w_gate", "w_up"):
        W = np.asarray(inp[f"L0_ffn_{nm}"], np.float32)[:, _PERM]
        a = W.reshape(ET, 128, 16, 512).transpose(2, 1, 0, 3)
        out[nm] = np.ascontiguousarray(a.reshape(16 * 128, ET * 512))
    W = np.asarray(inp["L0_ffn_w_down"], np.float32)[_PERM, :]
    a = W.reshape(64, 128, 2, 1024).transpose(2, 1, 0, 3)
    out["wd"] = np.ascontiguousarray(a.reshape(2 * 128, 64 * 1024))
    for nm in ("bq", "bk"):
        out[nm] = np.ascontiguousarray(
            np.asarray(inp[f"L0_attn_{nm}"], np.float32).reshape(H, 128).T)
    out["bv_b"] = np.ascontiguousarray(np.broadcast_to(
        np.asarray(inp["L0_attn_bv"], np.float32), (128, E)))
    out["bo_b"] = np.ascontiguousarray(np.broadcast_to(
        np.asarray(inp["L0_attn_bo"], np.float32), (128, E)))
    out["ident"] = np.eye(128, dtype=ml_dtypes.bfloat16)
    return out


def _host_tables(half):
    out = {}
    tb = np.full((13 * 4 * 32, 128), NEG, np.float64)
    for h in range(SMALL_H, H):
        for j in range(NCHUNK):
            q0 = (2 * j + half) * 512
            qC, qmax = q0 + 255, q0 + 511
            for kt in range(1024 * (j + 1) // 128):
                k = kt * 128 + np.arange(128)
                v = SLOPES[h] * (k - qC)
                v[k > qmax] = NEG
                tb[((h - 3) * 4 + j) * 32 + kt] = v
    out["tbig"] = np.ascontiguousarray(tb.T.astype(np.float32))
    ts = np.full((3 * 16 * 32, 128), NEG, np.float64)
    for h in range(SMALL_H):
        for cc in range(16):
            j, ss = cc // 4, cc % 4
            q0 = (2 * j + half) * 512 + ss * 128
            qC, qmax = q0 + 63, q0 + 127
            for kt in range(1024 * (j + 1) // 128):
                k = kt * 128 + np.arange(128)
                v = SLOPES[h] * (k - qC)
                v[k > qmax] = NEG
                ts[(h * 16 + cc) * 32 + kt] = v
    out["tsm"] = np.ascontiguousarray(ts.T.astype(np.float32))
    p = np.arange(128)[:, None]
    jq = np.arange(512)[None, :]
    m = np.zeros((8, 128, 512), np.float32)
    for s in range(8):
        dp = 128 * s - (512 if half == 1 else 0)
        m[s] = (p + dp <= jq)
    out["m512"] = np.ascontiguousarray(
        m.transpose(1, 0, 2).reshape(128, 8 * 512).astype(ml_dtypes.bfloat16))
    jq = np.arange(128)[None, :]
    m = np.zeros((4, 8, 128, 128), np.float32)
    for ss in range(4):
        for s in range(8):
            dp = 128 * (s - ss) - (512 if half == 1 else 0)
            m[ss, s] = (p + dp <= jq)
    out["m128"] = np.ascontiguousarray(
        m.transpose(2, 0, 1, 3).reshape(128, 4 * 8 * 128).astype(ml_dtypes.bfloat16))
    return out



class Sched:
    ENGS = ("tensor", "vector", "scalar", "gpsimd", "sync")

    def __init__(self, nc, es):
        self.nc = nc
        self.es = es
        self.q = {e: [] for e in self.ENGS}
        self.sems = {}
        for nm in ("te", "ve", "se", "ge"):
            self.sems[nm] = [es.enter_context(nc.semaphore(nm)), 0]

    def sem(self, nm):
        if nm not in self.sems:
            self.sems[nm] = [self.es.enter_context(self.nc.semaphore(nm)), 0]
        return nm

    def val(self, s):
        return self.sems[s][1]

    def emit(self, eng, fn):
        self.q[eng].append(fn)

    def wait(self, eng, s, v):
        if v <= 0:
            return
        assert v <= self.sems[s][1], f"wait on future {s}={v} > {self.sems[s][1]}"
        h = self.sems[s][0]
        self.emit(eng, lambda e, h=h, v=v: e.wait_ge(h, v))

    def waits(self, eng, toks):
        best = {}
        for t in toks:
            if t is None:
                continue
            s, v = t
            if v > best.get(s, 0):
                best[s] = v
        for s, v in best.items():
            self.wait(eng, s, v)

    def op(self, eng, fn, sem=None, k=1):
        if sem is None:
            self.emit(eng, lambda e, fn=fn: fn(e))
            return None
        h = self.sems[sem][0]

        def run(e, fn=fn, h=h, k=k):
            fn(e).then_inc(h, k)
        self.emit(eng, run)
        self.sems[sem][1] += k
        return (sem, self.sems[sem][1])

    def drain(self, eng):
        self.emit(eng, lambda e: e.drain())

    def barrier(self, skip=()):
        cur = {s: self.sems[s][1] for s in self.sems if s not in skip}
        for eng in self.ENGS:
            for s, v in cur.items():
                self.wait(eng, s, v)

    def finalize(self, block):
        for eng in self.ENGS:
            lst = self.q[eng]
            dec = getattr(block, eng)

            def body(e, lst=lst):
                for fn in lst:
                    fn(e)
            dec(body)



def build_program():
    nc = bass.Bass("TRN2", target_bir_lowering=False)
    es = contextlib.ExitStack()
    p = Sched(nc, es)

    def dma(eng, stream, dst, src):
        return p.op(eng, lambda e, d=dst, s=src: e.dma_start(d, s),
                    p.sem("D" + stream), 16)

    def dram_in(nm, shape, dt=F32):
        return nc.dram_tensor(nm, list(shape), dt, kind="ExternalInput").ap()

    x_own = dram_in("x_own", (NOWN, E))
    wqT = dram_in("wqT", (H * 128, ET * 128))
    wkT = dram_in("wkT", (H * 128, ET * 128))
    wvT = dram_in("wvT", (4 * 128, ET * 512))
    woT = dram_in("woT", (H * 128, E))
    wg_h = dram_in("w_gate", (16 * 128, ET * 512))
    wu_h = dram_in("w_up", (16 * 128, ET * 512))
    wd_h = dram_in("wd", (2 * 128, 64 * 1024))
    bq_h = dram_in("bq", (128, H))
    bk_h = dram_in("bk", (128, H))
    bvb_h = dram_in("bv_b", (128, E))
    bob_h = dram_in("bo_b", (128, E))
    id_h = dram_in("ident", (128, 128), BF16)
    tbig_h = dram_in("tbig", (128, 13 * 4 * 32))
    tsm_h = dram_in("tsm", (128, 3 * 16 * 32))
    m512_h = dram_in("m512", (128, 8 * 512), BF16)
    m128_h = dram_in("m128", (128, 4 * 8 * 128), BF16)

    y_h = nc.dram_tensor("y", [NOWN, E], F32, kind="ExternalOutput").ap()

    K_own = nc.dram_tensor("K_own", [H * 128, NOWN], BF16, kind="Internal").ap()
    V_own = nc.dram_tensor("V_own", [NOWN, E], BF16, kind="Internal").ap()
    K_gth = nc.dram_tensor("K_gth", [2 * H * 128, NOWN], BF16, kind="Internal").ap()
    V_gth = nc.dram_tensor("V_gth", [2 * NOWN, E], BF16, kind="Internal").ap()
    Q_sp = nc.dram_tensor("Q_sp", [H * 128, NOWN], BF16, kind="Internal").ap()
    Hsp = nc.dram_tensor("Hsp", [I, NOWN], BF16, kind="Internal").ap()
    Fsp = nc.dram_tensor("Fsp", [NOWN, E], F32, kind="Internal").ap()

    MBASE = 16512
    MEND = 229344
    sizes = {F32: 4, BF16: 2}

    def at(nm, cols, dt, off):
        assert off >= MBASE and off + cols * sizes[dt] <= MEND, (nm, off)
        return nc.alloc_sbuf_tensor_at(nm, [128, cols], dt, offset=off)

    o = MBASE
    ident = at("identS", 128, BF16, o); o += 256
    bqc = at("bqc", H, F32, o); o += 64
    bkc = at("bkc", H, F32, o); o += 64
    tbig = at("tbigS", 1664, F32, o); o += 1664 * 4
    tsm = at("tsmS", 1536, F32, o); o += 1536 * 4
    m512 = at("m512S", 4096, BF16, o); o += 8192
    m128 = at("m128S", 4096, BF16, o); o += 8192
    O_R = (o + 31) // 32 * 32
    O_W = O_R + 131072
    assert O_W + 52000 <= MEND, O_W

    xnT_o = at("xnT_o", ET * NOWN, BF16, O_R)
    wgB = at("wgB", ET * 512, BF16, O_R + 65536)
    wuB = at("wuB", ET * 512, BF16, O_R + 65536 + 16384)
    hstg2 = [at(f"hstg{i}", 4 * NOWN, BF16, O_R + 65536 + 32768 + i * 16384)
             for i in range(2)]
    wd_s = at("wd_s", 64 * 1024, BF16, O_R)
    avT = at("avT", H * NOWN, BF16, O_R)
    o5 = O_R + 65536
    kbuf = [at(f"kbuf{i}", S, BF16, o5 + i * 8192) for i in range(2)]
    o5 += 16384
    vbuf = [at(f"vbuf{i}", 32 * 130, BF16, o5 + i * 8320) for i in range(2)]
    o5 += 16640
    qbuf = [at(f"qbuf{i}", 2048, BF16, o5 + i * 4096) for i in range(2)]
    o5 += 8192
    ptb = [at(f"ptb{i}", 512, BF16, o5 + i * 1024) for i in range(6)]
    o5 += 6144
    zin = at("zin", 16, F32, o5); o5 += 64
    woT_s = at("woT_s", H * E, BF16, O_R + 65536)

    xb = [at(f"xb{i}", E, F32, O_W + i * 8192) for i in range(4)]
    xnb2 = [at(f"xnb{i}", E, BF16, O_W + 32768 + i * 4096) for i in range(2)]
    xnb = xnb2[0]
    stats = at("stats", 128, F32, O_W + 40960)
    O_P2 = O_R + 65536
    wkb = [at(f"wkb{i}", ET * 128, BF16, O_P2 + i * 4096) for i in range(2)]
    kout = [at(f"kout{i}", S, BF16, O_P2 + 8192 + i * 8192) for i in range(2)]
    wvb = [at(f"wvb{i}", ET * 512, BF16,
              O_P2 + (24576 if i == 0 else 49152)) for i in range(2)]
    vout = [at(f"vout{i}", 512, BF16, O_P2 + 40960 + i * 1024)
            for i in range(2)]
    bvb_s = at("bvb_s", E, BF16, O_P2 + 43008)
    wgA = at("wgA", ET * 512, BF16, O_W)
    wuA = at("wuA", ET * 512, BF16, O_W + 16384)
    hbuf = at("hbuf", 1536, F32, O_W + 32768)
    hsp2 = [at(f"hsp{i}", 512, BF16, O_W + 38912 + i * 1024)
            for i in range(2)]
    sps = [at(f"sp{i}", 512, F32, O_W + 40960 + i * 2048) for i in range(4)]
    spm = at("spm", 512, F32, O_W + 49152)
    htb = [at(f"htb{i}", 64 * 128, BF16, O_W + i * 16384) for i in range(2)]
    fout = at("fout", 1024, F32, O_W + 32768)
    bob16 = at("bob16", E, BF16, O_W + 36864)
    xt4 = [at(f"xt4{i}", 1024, F32, O_W + 40960 + i * 4096) for i in range(2)]
    ft = [at(f"ft{i}", E, F32, O_W + 16384 + i * 8192) for i in range(2)]
    ob = [at(f"ob{i}", E, F32, O_W + 32768 + i * 8192) for i in range(2)]

    pslot = [es.enter_context(nc.psum_tensor(f"ps{i}", [128, 512], F32))
             for i in range(4)]
    pav = [es.enter_context(nc.psum_tensor(f"pav{i}", [128, 512], F32))
           for i in range(4)]

    slot_free = [None] * 4
    slot_rr = [0]
    slot_n = [4]

    def acquire_slot(extra=()):
        s = slot_rr[0]
        slot_rr[0] = (s + 1) % slot_n[0]
        toks = [slot_free[s]] + list(extra)
        return s, toks

    c_toks = [dma("sync", "c0", ident[:], id_h[:, :]),
              dma("sync", "c1", tbig[:], tbig_h[:, :]),
              dma("sync", "c2", tsm[:], tsm_h[:, :]),
              dma("sync", "c3", m512[:], m512_h[:, :]),
              dma("sync", "c4", m128[:], m128_h[:, :])]

    def ln_phase(x_src, ntile, xnT_h, ncols):
        grp_done = []
        x_free = [None] * 4
        xn_free = [None, None]
        for g4 in range(ntile // 4):
            dtoks = []
            for i in range(4):
                t = g4 * 4 + i
                p.waits("sync", [x_free[i]])
                dtoks.append(dma("sync", f"xb{i}", xb[i][:, :],
                                 x_src[t * 128:(t + 1) * 128, :]))
            p.drain("vector")
            for i in range(4):
                p.waits("vector", [dtoks[i]])
                for c in range(4):
                    p.op("vector", lambda e, i=i, c=c: e.bn_stats(
                        stats[:, i * 24 + c * 6: i * 24 + (c + 1) * 6],
                        xb[i][:, c * 512:(c + 1) * 512]))
            p.drain("vector")
            for i in range(4):
                p.op("vector", lambda e, i=i: e.bn_aggr(
                    stats[:, 96 + 2 * i: 98 + 2 * i],
                    stats[:, i * 24:(i + 1) * 24]))
            p.drain("vector")
            mean = stats[:, 96:104:2]
            var = stats[:, 97:104:2]
            A = stats[:, 104:108]
            R0 = stats[:, 108:112]
            T = stats[:, 112:116]
            RS = stats[:, 116:120]
            NM = stats[:, 120:124]
            SQ = stats[:, 124:128]
            p.op("vector", lambda e: e.tensor_scalar(A, var, 1e-5, None, ALU.add))
            p.drain("vector")
            rtok = p.op("vector", lambda e: e.reciprocal(R0, A), "ve")
            p.waits("scalar", [rtok])
            stok = p.op("scalar", lambda e: e.activation(SQ, R0, AF.Sqrt), "se")
            p.waits("vector", [stok])
            p.op("vector", lambda e: e.tensor_tensor(T, SQ, SQ, ALU.mult))
            p.drain("vector")
            p.op("vector", lambda e: e.tensor_tensor(T, T, A, ALU.mult))
            p.drain("vector")
            p.op("vector", lambda e: e.tensor_scalar(T, T, -0.5, 1.5,
                                                     ALU.mult, ALU.add))
            p.drain("vector")
            p.op("vector", lambda e: e.tensor_tensor(RS, SQ, T, ALU.mult))
            p.drain("vector")
            p.op("vector", lambda e: e.scalar_tensor_tensor(
                NM, mean, -1.0, RS, ALU.mult, ALU.mult))
            p.drain("vector")
            for i in range(4):
                t = g4 * 4 + i
                sl2 = i % 2
                p.waits("vector", [xn_free[sl2]])
                xn_tok = p.op("vector", lambda e, i=i, sl2=sl2: e.tensor_scalar(
                    xnb2[sl2][:], xb[i][:], RS[:, i:i + 1], NM[:, i:i + 1],
                    ALU.mult, ALU.add), "ve")
                x_free[i] = xn_tok
                for g in range(4):
                    s, toks = acquire_slot([xn_tok, c_toks[0]])
                    p.waits("tensor", toks)
                    mt = None
                    for ii in range(4):
                        et = g * 4 + ii
                        mt = p.op("tensor", lambda e, s=s, ii=ii, et=et, sl2=sl2:
                                  e.matmul(
                            pslot[s][:, ii * 128:(ii + 1) * 128],
                            xnb2[sl2][:, et * 128:(et + 1) * 128], ident[:],
                            start=True, stop=True),
                            "te" if ii == 3 else None)
                    p.waits("scalar", [mt])
                    dst = bass.AP(xnT_h, (g * 4) * ncols + t * 128,
                                  [[ET * ncols, 128], [ncols, 4], [1, 128]])
                    ev = p.op("scalar", lambda e, s=s, d=dst: e.copy(
                        d, pslot[s][:]), "se")
                    slot_free[s] = ev
                xn_free[sl2] = ("te", p.val("te"))
            grp_done.append(("se", p.val("se")))
        return grp_done

    ln_grp = ln_phase(x_own, NOWN // 128, xnT_o, NOWN)

    ck = dma("sync", "cbk", bkc[:, :], bk_h[:, :])
    cv = dma("gpsimd", "cbv", bvb_s[:, :], bvb_h[:, :])

    wk_tok = [None, None]
    wk_used = [None, None]
    kout_free = [None, None]
    for h in range(H):
        sl = h % 2
        p.waits("gpsimd", [wk_used[sl]])
        wk_tok[sl] = dma("gpsimd", f"wk{sl}", wkb[sl][:, :], wkT[h * 128:(h + 1) * 128, :])
        evs = []
        for c8 in range(4):
            s, toks = acquire_slot([wk_tok[sl], ln_grp[c8]])
            p.waits("tensor", toks)
            mm = None
            for et in range(ET):
                mm = p.op("tensor", lambda e, s=s, et=et, c8=c8, sl=sl: e.matmul(
                    pslot[s][:], wkb[sl][:, et * 128:(et + 1) * 128],
                    xnT_o[:, et * NOWN + c8 * 512: et * NOWN + c8 * 512 + 512],
                    start=(et == 0), stop=(et == ET - 1)),
                    "te" if et == ET - 1 else None)
            w = [mm, ck] + ([kout_free[sl]] if c8 == 0 else [])
            p.waits("vector", w)
            ev = p.op("vector", lambda e, s=s, sl=sl, c8=c8, h=h: e.tensor_scalar(
                kout[sl][:, c8 * 512:(c8 + 1) * 512], pslot[s][:],
                bkc[:, h:h + 1], None, ALU.add), "ve")
            slot_free[s] = ev
            evs.append(ev)
        wk_used[sl] = ("te", p.val("te"))
        p.waits("scalar", evs)
        kout_free[sl] = dma("scalar", f"ko{sl}",
                            K_own[h * 128:(h + 1) * 128, :],
                            kout[sl][:, 0:NOWN])

    wv_used = [None, None]
    wv_tok = [None, None]
    vout_free = [None, None]
    for g4 in range(4):
        wsl = g4 % 2
        p.waits("gpsimd", [wv_used[wsl]])
        wv_tok[wsl] = dma("gpsimd", f"wv{wsl}", wvb[wsl][:, :],
                          wvT[g4 * 128:(g4 + 1) * 128, :])
        for tt in range(16):
            vsl = tt % 2
            s, toks = acquire_slot([wv_tok[wsl], ln_grp[tt // 4]])
            p.waits("tensor", toks)
            mm = None
            for et in range(ET):
                mm = p.op("tensor", lambda e, s=s, et=et, tt=tt, wsl=wsl: e.matmul(
                    pslot[s][:],
                    xnT_o[:, et * NOWN + tt * 128: et * NOWN + tt * 128 + 128],
                    wvb[wsl][:, et * 512:(et + 1) * 512],
                    start=(et == 0), stop=(et == ET - 1)),
                    "te" if et == ET - 1 else None)
            p.waits("vector", [mm, cv, vout_free[vsl]])
            ev = p.op("vector", lambda e, s=s, vsl=vsl, g4=g4: e.tensor_tensor(
                vout[vsl][:], pslot[s][:],
                bvb_s[:, g4 * 512:(g4 + 1) * 512], ALU.add), "ve")
            slot_free[s] = ev
            p.waits("scalar", [ev])
            vout_free[vsl] = dma(
                "scalar", f"vo{vsl}",
                V_own[tt * 128:(tt + 1) * 128, g4 * 512:(g4 + 1) * 512],
                vout[vsl][:, :])
        wv_used[wsl] = ("te", p.val("te"))

    cq = dma("sync", "cbq", bqc[:, :], bq_h[:, :])
    wq_tok = [None, None]
    wq_used = list(wk_used)
    qout_free = list(kout_free)
    for h in range(H):
        sl = h % 2
        p.waits("gpsimd", [wq_used[sl]])
        wq_tok[sl] = dma("gpsimd", f"wk{sl}", wkb[sl][:, :],
                         wqT[h * 128:(h + 1) * 128, :])
        evs = []
        for c4 in range(4):
            s, toks = acquire_slot([wq_tok[sl], ln_grp[c4]])
            p.waits("tensor", toks)
            mm = None
            for et in range(ET):
                mm = p.op("tensor", lambda e, s=s, et=et, c4=c4, sl=sl: e.matmul(
                    pslot[s][:], wkb[sl][:, et * 128:(et + 1) * 128],
                    xnT_o[:, et * NOWN + c4 * 512: et * NOWN + c4 * 512 + 512],
                    start=(et == 0), stop=(et == ET - 1)),
                    "te" if et == ET - 1 else None)
            w = [mm, cq] + ([qout_free[sl]] if c4 == 0 else [])
            p.waits("vector", w)
            ev = p.op("vector", lambda e, s=s, sl=sl, c4=c4, h=h: e.tensor_scalar(
                kout[sl][:, c4 * 512:(c4 + 1) * 512], pslot[s][:],
                bqc[:, h:h + 1], RD, ALU.add, ALU.mult), "ve")
            slot_free[s] = ev
            evs.append(ev)
        wq_used[sl] = ("te", p.val("te"))
        p.waits("scalar", evs)
        qout_free[sl] = dma("scalar", f"ko{sl}",
                            Q_sp[h * 128:(h + 1) * 128, :], kout[sl][:, 0:NOWN])
    p.waits("gpsimd", [("D" + s_, p.val("D" + s_))
                       for s_ in ("ko0", "ko1", "vo0", "vo1")])
    cc_sem = p.sem("cc")
    rg = [[0, 1], [2, 3], [4, 5], [6, 7]]
    cc_tok = None
    for cin_, cout_ in ((K_own, K_gth), (V_own, V_gth)):
        cc_tok = p.op("gpsimd", lambda e, a=cin_, b=cout_, rg=rg:
                      e.collective_compute(
            "AllGather", ALU.bypass, replica_groups=rg,
            ins=[a.opt()], outs=[b.opt()]), "cc", 1)
    p.barrier(skip=("cc",))

    wg_bufs = [(wgA, wuA), (wgB, wuB)]
    wg_used = [None, None]
    hstg_free = [None, None]
    hsp_free = [None, None]
    pav_free3 = [None, None]
    stage_evs = {}
    pend3 = [None]

    def p3_flush():
        if pend3[0] is None:
            return
        sp_tok, ic_, tt_ = pend3[0]
        pend3[0] = None
        b3 = tt_ % 2
        p.waits("tensor", [sp_tok, pav_free3[b3]])
        mt = None
        for i in range(4):
            mt = p.op("tensor", lambda e, b3=b3, i=i, tt_=tt_: e.matmul(
                pav[b3][:, i * 128:(i + 1) * 128],
                hsp2[tt_ % 2][:, i * 128:(i + 1) * 128], ident[:],
                start=True, stop=True), "te" if i == 3 else None)
        hsp_free[tt_ % 2] = ("te", p.val("te"))
        w = [mt] + ([hstg_free[ic_ % 2]] if tt_ == 0 else [])
        p.waits("scalar", w)
        dst = bass.AP(hstg2[ic_ % 2], tt_ * 128,
                      [[4 * NOWN, 128], [NOWN, 4], [1, 128]])
        ev = p.op("scalar", lambda e, b3=b3, d=dst: e.copy(d, pav[b3][:]), "se")
        pav_free3[b3] = ev
        stage_evs.setdefault(ic_, []).append(ev)
        if tt_ == 15:
            p.waits("sync", stage_evs[ic_])
            hdst = bass.AP(Hsp.tensor, (ic_ * 512) * NOWN,
                           [[NOWN, 128], [128 * NOWN, 4], [1, NOWN]])
            hstg_free[ic_ % 2] = dma("sync", f"hs{ic_ % 2}", hdst,
                                     hstg2[ic_ % 2][:, :])

    for ic in range(16):
        sl = ic % 2
        p.waits("gpsimd", [wg_used[sl]])
        t1 = dma("gpsimd", f"wg{sl}", wg_bufs[sl][0][:, :],
                 wg_h[ic * 128:(ic + 1) * 128, :])
        t2 = dma("gpsimd", f"wg{sl}", wg_bufs[sl][1][:, :],
                 wu_h[ic * 128:(ic + 1) * 128, :])
        for tt in range(16):
            sg = tt % 2
            su = 2 + tt % 2
            p.waits("tensor", [slot_free[sg], t2])
            mg = None
            for et in range(ET):
                mg = p.op("tensor", lambda e, sg=sg, et=et, tt=tt, sl=sl: e.matmul(
                    pslot[sg][:],
                    xnT_o[:, et * NOWN + tt * 128: et * NOWN + tt * 128 + 128],
                    wg_bufs[sl][0][:, et * 512:(et + 1) * 512],
                    start=(et == 0), stop=(et == ET - 1)),
                    "te" if et == ET - 1 else None)
            p.waits("tensor", [slot_free[su]])
            mu_ = None
            for et in range(ET):
                mu_ = p.op("tensor", lambda e, su=su, et=et, tt=tt, sl=sl: e.matmul(
                    pslot[su][:],
                    xnT_o[:, et * NOWN + tt * 128: et * NOWN + tt * 128 + 128],
                    wg_bufs[sl][1][:, et * 512:(et + 1) * 512],
                    start=(et == 0), stop=(et == ET - 1)),
                    "te" if et == ET - 1 else None)
            p3_flush()
            p.waits("scalar", [mg])
            sg_tok = p.op("scalar", lambda e, sg=sg: e.activation(
                hbuf[:, 0:512], pslot[sg][:], AF.Silu), "se")
            slot_free[sg] = sg_tok
            p.waits("vector", [sg_tok, mu_])
            p.drain("vector")
            hm = p.op("vector", lambda e, su=su: e.tensor_tensor(
                hbuf[:, 512:1024], hbuf[:, 0:512], pslot[su][:], ALU.mult),
                "ve")
            slot_free[su] = hm
            A = [hbuf[:, 512 + j * 128: 512 + (j + 1) * 128] for j in range(4)]
            p.waits("scalar", [hm])
            ab = p.op("scalar", lambda e: e.activation(
                hbuf[:, 1024:1536], hbuf[:, 512:1024], AF.Abs), "se")
            Aa = [hbuf[:, 1024 + j * 128: 1024 + (j + 1) * 128] for j in range(4)]
            p.waits("vector", [ab])
            p.drain("vector")
            p.op("vector", lambda e, A0=Aa[0], A1=Aa[1]: e.tensor_tensor(
                sps[0][:, 0:128], A0, A1, ALU.max))
            p.op("vector", lambda e, A0=Aa[0], A1=Aa[1]: e.tensor_tensor(
                sps[1][:, 0:128], A0, A1, ALU.min))
            p.op("vector", lambda e, A2=Aa[2], A3=Aa[3]: e.tensor_tensor(
                sps[2][:, 0:128], A2, A3, ALU.max))
            p.op("vector", lambda e, A2=Aa[2], A3=Aa[3]: e.tensor_tensor(
                sps[3][:, 0:128], A2, A3, ALU.min))
            p.drain("vector")
            p.op("vector", lambda e: e.tensor_tensor(
                sps[0][:, 128:256], sps[0][:, 0:128], sps[2][:, 0:128],
                ALU.min))
            p.op("vector", lambda e: e.tensor_tensor(
                sps[1][:, 128:256], sps[1][:, 0:128], sps[3][:, 0:128],
                ALU.max))
            p.drain("vector")
            p.op("vector", lambda e: e.tensor_tensor(
                sps[0][:, 256:384], sps[0][:, 128:256], sps[1][:, 128:256],
                ALU.max))
            p.drain("vector")
            sec = sps[0][:, 256:384]
            for j in range(4):
                p.op("vector", lambda e, j=j, a=Aa[j], sec=sec: e.tensor_tensor(
                    spm[:, j * 128:(j + 1) * 128], a, sec, ALU.is_ge))
            p.drain("vector")
            p.waits("vector", [hsp_free[tt % 2]])
            sp_tok = None
            for j in range(4):
                sp_tok = p.op("vector", lambda e, j=j, a=A[j], tt=tt:
                              e.tensor_tensor(
                    hsp2[tt % 2][:, j * 128:(j + 1) * 128], a,
                    spm[:, j * 128:(j + 1) * 128], ALU.mult),
                    "ve" if j == 3 else None)
            pend3[0] = (sp_tok, ic, tt)
        wg_used[sl] = ("te", p.val("te"))
    p3_flush()
    wd_pre = []
    p.waits("gpsimd", [("te", p.val("te"))])
    for b in range(8):
        if b >= 2:
            p.waits("gpsimd", [wd_pre[b - 2]])
        if b == 6:
            p.waits("gpsimd", [hstg_free[0], hstg_free[1]])
        wd_pre.append(dma("gpsimd", f"wd{b}",
                          wd_s[:, b * 8192:(b + 1) * 8192],
                          wd_h[0:128, b * 8192:(b + 1) * 8192]))
    def p4_loads(tt, half_e):
        sl = tt % 2
        xtok = dma("sync", f"x4{sl}", xt4[sl][:, :],
                   x_own[tt * 128:(tt + 1) * 128,
                         half_e * 1024:(half_e + 1) * 1024])
        toks = []
        for pc in range(8):
            srcp = bass.AP(Hsp.tensor, (pc * 8 * 128) * NOWN + tt * 128,
                           [[NOWN, 128], [128 * NOWN, 8], [1, 128]])
            dstp = htb[sl][:, pc * 8 * 128:(pc + 1) * 8 * 128]
            toks.append(dma("sync", f"ht{sl}", dstp, srcp))
        return xtok, toks

    pre4 = {}
    p.waits("sync", [("te", p.val("te")), ("ve", p.val("ve")),
                     hstg_free[0], hstg_free[1]])
    for tt in (0, 1):
        pre4[tt] = p4_loads(tt, 0)
    p.barrier(skip=tuple(f"Dwd{b}" for b in range(8))
              + ("Dx40", "Dx41", "Dht0", "Dht1"))

    bt = dma("gpsimd", "cbo", bob16[:, :], bob_h[:, :])
    ht_used = [None, None]
    xt4_used = [None, None]
    fout_free = None
    wd_ok = {}
    for half_e in range(2):
        if half_e:
            wtoks = []
            for b in range(8):
                p.waits("gpsimd", [wd_ok[b]] + ([wtoks[b - 2]] if b >= 2 else []))
                wtoks.append(dma("gpsimd", f"wd{b}",
                                 wd_s[:, b * 8192:(b + 1) * 8192],
                                 wd_h[128:256, b * 8192:(b + 1) * 8192]))
        else:
            wtoks = wd_pre
        for tt in range(16):
            sl = tt % 2
            if half_e == 0 and tt in pre4:
                xtok, toks = pre4[tt]
            else:
                p.waits("sync", [ht_used[sl], xt4_used[sl]])
                xtok, toks = p4_loads(tt, half_e)
            evs = []
            for ec in range(2):
                s, stoks = acquire_slot([toks[-1], wtoks[0]])
                p.waits("tensor", stoks)
                mm = None
                for it in range(64):
                    if it % 8 == 0 and it > 0:
                        p.waits("tensor", [wtoks[it // 8]])
                    fin8 = (half_e == 0 and tt == 15 and ec == 1
                            and it % 8 == 7)
                    mm = p.op("tensor", lambda e, s=s, it=it, ec=ec, sl=sl: e.matmul(
                        pslot[s][:], htb[sl][:, it * 128:(it + 1) * 128],
                        wd_s[:, it * 1024 + ec * 512: it * 1024 + ec * 512 + 512],
                        start=(it == 0), stop=(it == 63)),
                        "te" if (it == 63 or fin8) else None)
                    if fin8:
                        wd_ok[it // 8] = ("te", p.val("te"))
                w = [mm, bt, xtok] + ([fout_free] if ec == 0 else [])
                p.waits("vector", w)
                p.op("vector", lambda e, s=s, ec=ec, he=half_e:
                     e.tensor_tensor(
                    fout[:, ec * 512:(ec + 1) * 512], pslot[s][:],
                    bob16[:, he * 1024 + ec * 512: he * 1024 + ec * 512 + 512],
                    ALU.add))
                p.drain("vector")
                ev = p.op("vector", lambda e, sl=sl, ec=ec: e.tensor_tensor(
                    fout[:, ec * 512:(ec + 1) * 512],
                    fout[:, ec * 512:(ec + 1) * 512],
                    xt4[sl][:, ec * 512:(ec + 1) * 512], ALU.add), "ve")
                slot_free[s] = ev
                evs.append(ev)
            ht_used[sl] = ("te", p.val("te"))
            xt4_used[sl] = evs[-1]
            p.waits("scalar", evs)
            fout_free = dma(
                "scalar", "fo",
                Fsp[tt * 128:(tt + 1) * 128,
                    half_e * 1024:(half_e + 1) * 1024],
                fout[:, 0:1024])
    p.barrier()

    def win_start(h, q0v, small):
        W = math.ceil(60.0 / SLOPES[h])
        kt0 = max(0, (q0v - W) // 128)
        if small:
            kt0 &= ~3
        return kt0

    units = []
    for h in range(SMALL_H, H):
        for j in range(NCHUNK):
            units.append((h, j, 512, 1024 * (j + 1),
                          tbig, ((h - 3) * 4 + j) * 32, j * 512,
                          win_start(h, 1024 * j, False)))
    for h in range(SMALL_H):
        for cc in range(16):
            j = cc // 4
            units.append((h, cc, 128, 1024 * (j + 1),
                          tsm, (h * 16 + cc) * 32, cc * 128,
                          win_start(h, 1024 * j + 128 * (cc % 4), True)))
    slot_n[0] = 3
    slot_rr[0] = 0
    kv_used = [None, None]
    av_free = [None] * 4
    zin_free = [None] * 4
    xnb_free = [None] * 4
    ptb_free = [None] * 6
    ptsl_rr = [0]
    pending_norm = []
    pend_g = []

    def pump():
        fn, unit_first, fin = pend_g.pop(0)
        if unit_first:
            while pending_norm:
                pending_norm.pop(0)()
        fn()
        if fin is not None:
            fin()

    heads = []
    for u in units:
        if not heads or heads[-1][0] != u[0]:
            heads.append((u[0], []))
        heads[-1][1].append(u)
    for hi, (h, hunits) in enumerate(heads):
        sl = hi % 2
        p.waits("sync", [kv_used[sl], ("cc", p.val("cc"))])
        for g in range(S // 512):
            r, jj = g % 2, g // 2
            dma("sync", f"kv{sl}",
                kbuf[sl][:, g * 512:(g + 1) * 512],
                K_gth[r * 2048 + h * 128: r * 2048 + (h + 1) * 128,
                      jj * 512:(jj + 1) * 512])
            srcp = bass.AP(V_gth.tensor,
                           (r * 2048 + jj * 512) * E + h * 128,
                           [[E, 128], [128 * E, 4], [1, 128]])
            dstp = bass.AP(vbuf[sl], (g * 4) * 130,
                           [[32 * 130, 128], [130, 4], [1, 128]])
            dma("sync", f"kv{sl}", dstp, srcp)
        qtok = dma("sync", f"kv{sl}", qbuf[sl][:, :],
                   Q_sp[h * 128:(h + 1) * 128, :])
        p.waits("gpsimd", [qtok])
        ones_tok = p.op("gpsimd", lambda e, sl=sl: e.memset(
            bass.AP(vbuf[sl], 128, [[32 * 130, 128], [130, 32], [1, 1]]),
            1.0), "ge")
        for ux, (h_, cj, cw, kend, tb_s, tcol, ocol, kt0) in enumerate(hunits):
            nkt = kend // 128
            nqs = cw // 128
            if cw == 512:
                kt_groups = [[kt] for kt in range(kt0, nkt)]
            else:
                kt_groups = [list(range(g * 4, g * 4 + 4))
                             for g in range(kt0 // 4, nkt // 4)]
            first_av = [True]

            def mk_flush(grp_work, sl=sl, nqs=nqs, nkt=nkt, kt0=kt0,
                         qtok=qtok, ones_tok=ones_tok, first_av=first_av):
                def flushfn():
                    for kt, ptsl_, off, rdy in grp_work:
                        w = [rdy]
                        if first_av[0]:
                            w += [ones_tok, qtok] + av_free[0:nqs]
                            first_av[0] = False
                        p.waits("tensor", w)
                        for qs in range(nqs):
                            p.op("tensor", lambda e, qs=qs, ptsl=ptsl_, off=off,
                                 kt=kt, first=(kt == kt0), last=(kt == nkt - 1):
                                 e.matmul(
                                pav[qs][:, 0:129],
                                ptb[ptsl][:, off + qs * 128: off + qs * 128 + 128],
                                vbuf[sl][:, kt * 130:kt * 130 + 129],
                                start=first, stop=last),
                                "te" if qs == nqs - 1 else None)
                        ptb_free[ptsl_] = ("te", p.val("te"))
                return flushfn

            def make_norm_qs(qs, h=h, ocol=ocol):
                def normfn(av_fin):
                    p.waits("vector", [av_fin, zin_free[qs]])
                    rtok = p.op("vector", lambda e, qs=qs: e.reciprocal(
                        zin[:, qs:qs + 1],
                        pav[qs][:, 128:129]), "ve")
                    p.waits("vector", [xnb_free[qs]])
                    p.drain("vector")
                    ntok = p.op("vector", lambda e, qs=qs: e.tensor_scalar(
                        xnb[:, qs * 128:(qs + 1) * 128], pav[qs][:, 0:128],
                        zin[:, qs:qs + 1], None, ALU.mult), "ve")
                    zin_free[qs] = ntok
                    av_free[qs] = ntok
                    p.waits("tensor", [ntok, slot_free[3]])
                    mt = p.op("tensor", lambda e, qs=qs: e.matmul(
                        pslot[3][:, 0:128], xnb[:, qs * 128:(qs + 1) * 128],
                        ident[:], start=True, stop=True), "te")
                    xnb_free[qs] = mt
                    p.waits("vector", [mt])
                    ev = p.op("vector", lambda e, h=h, ocol=ocol, qs=qs:
                              e.tensor_copy(
                        avT[:, h * NOWN + ocol + qs * 128:
                            h * NOWN + ocol + (qs + 1) * 128],
                        pslot[3][:, 0:128]), "ve")
                    slot_free[3] = ev
                return normfn

            def mk_fin(nqs=nqs, sl=sl, head_last=(ux == len(hunits) - 1),
                       make_norm_qs=make_norm_qs):
                def fin():
                    av_fin = ("te", p.val("te"))
                    for qs_ in range(nqs):
                        fnq = make_norm_qs(qs_)
                        pending_norm.append(
                            lambda fnq=fnq, av_fin=av_fin: fnq(av_fin))
                    if head_last:
                        kv_used[sl] = ("te", p.val("te"))
                return fin

            for gi, grp in enumerate(kt_groups):
                s, toks = acquire_slot([qtok])
                p.waits("tensor", toks)
                mm = None
                for kt in grp:
                    off = (kt - grp[0]) * cw if cw == 128 else 0
                    mm = p.op("tensor", lambda e, s=s, kt=kt, off=off, sl=sl,
                              cw=cw, ocol=ocol: e.matmul(
                        pslot[s][:, off:off + cw],
                        kbuf[sl][:, kt * 128:(kt + 1) * 128],
                        qbuf[sl][:, ocol:ocol + cw], start=True, stop=True),
                        "te" if kt == grp[-1] else None)
                if pending_norm:
                    pending_norm.pop(0)()
                if len(pend_g) >= 4:
                    pump()
                ptsl = ptsl_rr[0]
                ptsl_rr[0] = (ptsl + 1) % 6
                p.waits("scalar", [mm, ptb_free[ptsl], c_toks[1], c_toks[2]])
                kt_ready = {}
                kt_off = {}
                for kt in grp:
                    off = (kt - grp[0]) * cw if cw == 128 else 0
                    etok = p.op("scalar", lambda e, s=s, kt=kt, off=off, ptsl=ptsl,
                                tb_s=tb_s, tcol=tcol, cw=cw: e.activation(
                        ptb[ptsl][:, off:off + cw], pslot[s][:, off:off + cw],
                        AF.Exp, bias=tb_s[:, tcol + kt:tcol + kt + 1], scale=1.0),
                        "se")
                    kt_ready[kt] = etok
                    kt_off[kt] = (ptsl, off)
                slot_free[s] = kt_ready[grp[-1]]
                for kt in grp:
                    msl = kt - (nkt - 8)
                    if msl >= 0:
                        ptsl_, off = kt_off[kt]
                        if cw == 512:
                            mv = m512[:, msl * 512:(msl + 1) * 512]
                        else:
                            ss = cj % 4
                            mv = m128[:, (ss * 8 + msl) * 128:(ss * 8 + msl + 1) * 128]
                        p.waits("vector", [kt_ready[kt], c_toks[3], c_toks[4]])
                        kt_ready[kt] = p.op("vector", lambda e, ptsl=ptsl_, off=off,
                                            cw=cw, mv=mv: e.tensor_tensor(
                            ptb[ptsl][:, off:off + cw], ptb[ptsl][:, off:off + cw],
                            mv, ALU.mult), "ve")
                work = [(kt, kt_off[kt][0], kt_off[kt][1], kt_ready[kt])
                        for kt in grp]
                pend_g.append((mk_flush(work), gi == 0,
                               mk_fin() if gi == len(kt_groups) - 1 else None))
    while pend_g:
        pump()
    while pending_norm:
        pending_norm.pop(0)()
    slot_n[0] = 4
    slot_rr[0] = 0
    p.barrier()

    wo_toks = []
    for hg in range(4):
        srcp = bass.AP(woT.tensor, (hg * 4 * 128) * E,
                       [[E, 128], [128 * E, 4], [1, E]])
        wo_toks.append(dma("gpsimd", "wo", woT_s[:, hg * 4 * E:(hg + 1) * 4 * E],
                           srcp))
    ft_used = [None, None]
    ob_free = [None, None]
    out_dmas = []
    for qt in range(16):
        sl = qt % 2
        p.waits("sync", [ft_used[sl]])
        ftok = dma("sync", f"ft{sl}", ft[sl][:, :],
                   Fsp[qt * 128:(qt + 1) * 128, :])
        evs = []
        for ec in range(4):
            s, toks = acquire_slot([wo_toks[-1]])
            p.waits("tensor", toks)
            mm = None
            for h in range(H):
                mm = p.op("tensor", lambda e, s=s, h=h, qt=qt, ec=ec: e.matmul(
                    pslot[s][:],
                    avT[:, h * NOWN + qt * 128: h * NOWN + (qt + 1) * 128],
                    woT_s[:, h * E + ec * 512: h * E + ec * 512 + 512],
                    start=(h == 0), stop=(h == H - 1)),
                    "te" if h == H - 1 else None)
            w = [mm, ftok] + ([ob_free[sl]] if ec == 0 else [])
            p.waits("vector", w)
            ev = p.op("vector", lambda e, s=s, sl=sl, ec=ec: e.tensor_tensor(
                ob[sl][:, ec * 512:(ec + 1) * 512], pslot[s][:],
                ft[sl][:, ec * 512:(ec + 1) * 512], ALU.add), "ve")
            slot_free[s] = ev
            evs.append(ev)
        ft_used[sl] = evs[-1]
        p.waits("scalar", evs)
        otok = dma("scalar", f"ob{sl}", y_h[qt * 128:(qt + 1) * 128, :], ob[sl][:, :])
        ob_free[sl] = otok
        out_dmas.append(otok)
    p.waits("scalar", out_dmas)
    p.barrier()

    with nc.Block() as block:
        p.finalize(block)
    es.close()
    return nc


_CACHE = {}


def kernel(**inputs):
    if "nc" not in _CACHE:
        _CACHE["nc"] = build_program()
    nc = _CACHE["nc"]
    hw = _host_weights(inputs)
    x = np.asarray(inputs["L0_attn_x"], np.float32)
    tabs = [_host_tables(0), _host_tables(1)]
    in_maps = []
    for c in range(8):
        b, half = c // 2, c % 2
        x_own = np.concatenate(
            [x[b, (2 * j + half) * 512:(2 * j + half) * 512 + 512]
             for j in range(NCHUNK)], axis=0)
        in_maps.append({
            "x_own": np.ascontiguousarray(x_own),
            "wqT": hw["wqT"], "wkT": hw["wkT"], "wvT": hw["wvT"],
            "woT": hw["woT"],
            "w_gate": hw["w_gate"], "w_up": hw["w_up"], "wd": hw["wd"],
            "bq": hw["bq"], "bk": hw["bk"],
            "bv_b": hw["bv_b"], "bo_b": hw["bo_b"],
            "ident": hw["ident"],
            "tbig": tabs[half]["tbig"], "tsm": tabs[half]["tsm"],
            "m512": tabs[half]["m512"], "m128": tabs[half]["m128"],
        })
    res = run_bass_kernel_spmd(nc, in_maps, core_ids=list(range(8)))
    y = np.empty((B, S, E), np.float32)
    for c in range(8):
        b, half = c // 2, c % 2
        yc = np.asarray(res.results[c]["y"])
        for j in range(NCHUNK):
            g = 2 * j + half
            y[b, g * 512:(g + 1) * 512] = yc[j * 512:(j + 1) * 512]
    return y
